# version 23
# baseline (speedup 1.0000x reference)
import math
import numpy as np
import concourse.bass as bass
import concourse.mybir as mybir
from concourse.bass_utils import run_bass_kernel_spmd

F32 = mybir.dt.float32
BF16 = mybir.dt.bfloat16
F16 = mybir.dt.float16
AF = mybir.ActivationFunctionType
ALU = mybir.AluOpType
AX = mybir.AxisListType

N_CORES = 8
CPB = 4
B = 2
HDIM = 128
SHIFT = 4.0
EPS = 1e-5


class Ctr:
    __slots__ = ("name", "n", "h")

    def __init__(self, name):
        self.name = name
        self.n = 0
        self.h = None


class Emit:
    ENGINES = ("pe", "dve", "act", "pool", "sp")

    def __init__(self):
        self.q = {e: [] for e in self.ENGINES}
        self.ctrs = {}

    def ctr(self, name):
        c = Ctr(name)
        self.ctrs[name] = c
        return c

    def op(self, eng, fn, waits=(), inc=None, inc_k=1):
        ws = tuple((c, v) for (c, v) in waits if v > 0)
        m = None
        if inc is not None:
            inc.n += inc_k
            m = inc.n

        def run(e, _fn=fn, _ws=ws, _inc=inc, _k=inc_k):
            for (c, v) in _ws:
                e.wait_ge(c.h, v)
            ins = _fn(e)
            if _inc is not None and ins is not None:
                ins.then_inc(_inc.h, _k)
        self.q[eng].append(run)
        return m

    def wait(self, eng, waits):
        ws = tuple((c, v) for (c, v) in waits if v > 0)
        if not ws:
            return

        def run(e, _ws=ws):
            for (c, v) in _ws:
                e.wait_ge(c.h, v)
        self.q[eng].append(run)

    def raw(self, eng, fn):
        self.q[eng].append(fn)

    def barrier(self, exclude=()):
        snap = [(c, c.n) for c in self.ctrs.values()
                if c.n > 0 and c not in exclude]
        for e in self.ENGINES:
            self.wait(e, snap)


def build_program(T, C, n_arms=128, debug_taps=False):
    NH = C // HDIM
    HPC = NH // CPB
    Dc = HPC * HDIM
    CC = C // 128
    TPC = T // CPB
    NCH = 2
    TCH = T // NCH
    TB = min(512, TCH)
    NTB = TCH // TB
    QG = 512
    NQG = T // QG
    STG = QG // 128
    VDS = min(256, Dc)
    NVDS = Dc // VDS
    NHF = 2
    TH = TPC // NHF
    TTH = TH // 128
    DB3 = min(512, C)
    NDB = C // DB3
    SC = 1.0 / math.sqrt(HDIM)
    assert TCH % 128 == 0 and TCH % TB == 0 and T % QG == 0 and Dc % VDS == 0
    assert TH % 128 == 0 and C % DB3 == 0 and DB3 % 128 == 0

    nc = bass.Bass("TRN2", target_bir_lowering=False, debug=False)

    x_full = nc.dram_tensor("x_full", [T, C], F32, kind="ExternalInput")
    x_slice = nc.dram_tensor("x_slice", [TPC, C], F32, kind="ExternalInput")
    wq = nc.dram_tensor("wq", [Dc, C], F32, kind="ExternalInput")
    wk = nc.dram_tensor("wk", [Dc, C], F32, kind="ExternalInput")
    wv = nc.dram_tensor("wv", [Dc, C], F32, kind="ExternalInput")
    wo = nc.dram_tensor("wo", [C, C], F32, kind="ExternalInput")
    fw = nc.dram_tensor("fw", [C, C], F16, kind="ExternalInput")
    fb = nc.dram_tensor("fb", [1, C], F16, kind="ExternalInput")
    rsin = nc.dram_tensor("rsin", [128, T], F32, kind="ExternalInput")
    rcos = nc.dram_tensor("rcos", [128, T], F32, kind="ExternalInput")
    out = nc.dram_tensor("out", [TPC, C], F32, kind="ExternalOutput")

    q_scr = nc.dram_tensor("q_scr", [HPC, 128, T], BF16)
    k_scr = nc.dram_tensor("k_scr", [HPC, 128, T], BF16)
    v_scr = nc.dram_tensor("v_scr", [HPC, T, 128], BF16)
    a_in = nc.dram_tensor("a_in", [Dc, T], BF16)
    Dc2 = Dc // 2
    ag_outs = [nc.dram_tensor("ag_out0", [CPB * Dc2, T], BF16),
               nc.dram_tensor("ag_out1", [CPB * Dc2, T], BF16)]
    woT_d = nc.dram_tensor("woT_d", [C, C], BF16)
    fwT_d = nc.dram_tensor("fwT_d", [C, C], BF16)

    em = Emit()
    c_pe = em.ctr("c_pe"); c_dve = em.ctr("c_dve"); c_act = em.ctr("c_act")
    c_pool = em.ctr("c_pool"); c_cc = em.ctr("c_cc"); c_cc2 = em.ctr("c_cc2")
    d_stg = [em.ctr("d_stg0"), em.ctr("d_stg1")]
    d_tp = [em.ctr("d_tp0"), em.ctr("d_tp1")]
    d_xtp = [em.ctr("d_xtp0"), em.ctr("d_xtp1")]
    d_rope = em.ctr("d_rope")
    d_qk = [em.ctr("d_qk%d" % i) for i in range(4)]
    d_vs = [em.ctr("d_vs0"), em.ctr("d_vs1")]
    d_qh = [em.ctr("d_qh%d" % i) for i in range(3)]
    d_kh = [em.ctr("d_kh%d" % i) for i in range(3)]
    d_vh = [em.ctr("d_vh%d" % i) for i in range(3)]
    d_as = [em.ctr("d_as0"), em.ctr("d_as1")]
    d_at = [em.ctr("d_at0"), em.ctr("d_at1")]
    d_xl = em.ctr("d_xl")
    d_w3t = [em.ctr("d_w3t0"), em.ctr("d_w3t1")]
    d_w3s = [em.ctr("d_w3s0")]
    d_ps = [em.ctr("d_ps%d" % i) for i in range(3)]
    d_pw = [em.ctr("d_pw%d" % i) for i in range(3)]
    d_x2t = em.ctr("d_x2t")
    d_fb = em.ctr("d_fb")
    d_out = em.ctr("d_out")

    persist_ctx = []
    phase_ctx = []

    def alloc(cm, ctx_list):
        g = cm.__enter__()
        ctx_list.append(cm)
        return g

    def sbp(name, shape, dt, **kw):
        return alloc(nc.sbuf_tensor(name, shape, dt, **kw), persist_ctx)

    def sbf(name, shape, dt, **kw):
        return alloc(nc.sbuf_tensor(name, shape, dt, **kw), phase_ctx)

    def free_phase():
        while phase_ctx:
            phase_ctx.pop().__exit__(None, None, None)

    mask_tri = sbp("mask_tri", [128, 128], F32)
    kconst = sbp("kconst", [128, 3], F32)
    ones_cb = sbp("ones_cb", [128, 1], BF16)
    ones_rf = sbp("ones_rf", [1, 128], F32)
    ones_rh = sbp("ones_rh", [1, 128], F16)
    pA = alloc(nc.psum_tensor("pA", [128, 512], F32), persist_ctx)
    pB = alloc(nc.psum_tensor("pB", [128, 512], F32), persist_ctx)
    pC = alloc(nc.psum_tensor("pC", [128, 512], F32), persist_ctx)
    pD = alloc(nc.psum_tensor("pD", [128, 512], F32), persist_ctx)
    pE = alloc(nc.psum_tensor("pE", [128, 512], F32), persist_ctx)
    pF = alloc(nc.psum_tensor("pF", [128, 512], F32), persist_ctx)
    pG = alloc(nc.psum_tensor("pG", [128, 512], F32), persist_ctx)
    pH = alloc(nc.psum_tensor("pH", [128, 512], F32), persist_ctx)

    m0 = em.op("pool", lambda g: g.memset(mask_tri[:, :], 0.0), inc=c_pool)
    em.op("pool", lambda g: g.affine_select(
        out=mask_tri[:, :], in_=mask_tri[:, :], compare_op=ALU.is_ge,
        fill=-1e9, base=0, pattern=[[1, 128]], channel_multiplier=-1),
        waits=[(c_pool, m0)], inc=c_pool)
    em.op("pool", lambda g: g.memset(ones_cb[:, :], 1.0), inc=c_pool)
    em.op("pool", lambda g: g.memset(ones_rf[:, :], 1.0), inc=c_pool)
    em.op("pool", lambda g: g.memset(ones_rh[:, :], 1.0), inc=c_pool)
    em.op("pool", lambda g: g.memset(kconst[:, 0:1], -SHIFT), inc=c_pool)
    em.op("pool", lambda g: g.memset(kconst[:, 1:2], EPS), inc=c_pool)
    em.op("pool", lambda g: g.memset(kconst[:, 2:3], 1.0 / C), inc=c_pool)
    M_consts = c_pool.n

    stage = [sbf("stage0", [128, C], BF16), sbf("stage1", [128, C], BF16)]
    stage_free = [(c_pool, 0), (c_pool, 0)]
    stage_i = [0]

    def stage_load(src_fn):
        s = stage_i[0] % 2
        stage_i[0] += 1
        mset = em.op("pool", lambda g, _f=src_fn, _s=s: g.dma_start(
            out=stage[_s][:, :], in_=_f()), waits=[stage_free[s]],
            inc=d_stg[s], inc_k=16)
        return s, (d_stg[s], mset)

    xT = [sbf("xT0", [128, CC, TCH], BF16, is_transpose_dest=True)]
    srs = [sbf("srs0", [128, TCH], F32)]
    src_ = [sbf("src0", [128, TCH], F32)]
    wT = [sbf("wT0", [128, CC, 128], BF16, is_transpose_dest=True),
          sbf("wT1", [128, CC, 128], BF16, is_transpose_dest=True)]
    wT_free = [(c_pe, 0), (c_pe, 0)]
    wvT = [sbf("wvT0", [128, CC, VDS], BF16, is_transpose_dest=True)]
    wvT_free = [(c_pe, 0)]
    rot = sbf("rot", [128, TB], F32)
    qtmp = sbf("qtmp", [128, TB], F32)
    qkst = [sbf("qkst%d" % i, [128, TB], BF16) for i in range(4)]
    qkst_free = [(d_qk[i], 0) for i in range(4)]
    qk_i = [0]
    vst = [sbf("vst0", [128, VDS], BF16), sbf("vst1", [128, VDS], BF16)]
    vst_free = [(d_vs[0], 0), (d_vs[1], 0)]
    vs_i = [0]

    rope_last_use = [(c_dve, 0)]
    xT_free = [(c_pe, 0), (c_pe, 0)]
    QK_BANKS = [pA, pB, pE, pF]
    bank_qk_free = [(c_dve, 0)] * 4
    bank_v_free = [(c_dve, 0), (c_dve, 0)]
    qkb_i = [0]

    def build_xT_part(ch, tr0, tr1, marks):
        h = 0
        TOFF = ch * TCH
        for tr in range(tr0, tr1):
            s, rdy = stage_load(
                (lambda _o=TOFF + tr * 128: x_full[_o:_o + 128, :]))
            mxt = em.op("sp", lambda e, _s=s, _tr=tr, _h=h: e.dma_start(
                out=xT[_h][:, :, _tr * 128:(_tr + 1) * 128],
                in_=stage[_s][:, :], transpose=True),
                waits=[rdy] + ([xT_free[h]] if tr == 0 else []),
                inc=d_xtp[s], inc_k=16)
            stage_free[s] = (d_xtp[s], mxt)
            marks.append((d_xtp[s], mxt))

    NTR = TCH // 128
    xT_marks = []
    build_xT_part(0, 0, min(TB // 128, NTR), xT_marks)

    for ch in range(NCH):
        h = 0
        hr = 0
        TOFF = ch * TCH
        mr1 = em.op("act", lambda s, _o=TOFF, _h=hr: s.dma_start(
            out=srs[_h][:, :], in_=rsin[:, _o:_o + TCH]),
            waits=[rope_last_use[hr]], inc=d_rope, inc_k=16)
        mr2 = em.op("act", lambda s, _o=TOFF, _h=hr: s.dma_start(
            out=src_[_h][:, :], in_=rcos[:, _o:_o + TCH]),
            inc=d_rope, inc_k=16)
        rope_ready = (d_rope, mr2)
        if len(xT_marks) < NTR:
            build_xT_part(ch, len(xT_marks), NTR, xT_marks)

        def tb_ready(tb):
            return xT_marks[tb * (TB // 128):(tb + 1) * (TB // 128)]

        for widx, (wten, scr) in enumerate(((wq, q_scr), (wk, k_scr))):
            for dt in range(HPC):
                ws = (widx * HPC + dt) % 2
                s, rdy = stage_load(
                    (lambda _w=wten, _d=dt: _w[_d * 128:(_d + 1) * 128, :]))
                mwt = em.op("sp", lambda e, _s=s, _ws=ws: e.dma_start(
                    out=wT[_ws][:, :, :], in_=stage[_s][:, :], transpose=True),
                    waits=[rdy, wT_free[ws]], inc=d_tp[s], inc_k=16)
                stage_free[s] = (d_tp[s], mwt)
                wt_rdy = (d_tp[s], mwt)
                for tb in range(NTB):
                    bk = qkb_i[0] % 4
                    qkb_i[0] += 1
                    bank = QK_BANKS[bk]
                    mm = None
                    for cc in range(CC):
                        w = ([wt_rdy, bank_qk_free[bk]] + tb_ready(tb)
                             if cc == 0 else [])
                        mm = em.op("pe", lambda t, _ws=ws, _cc=cc, _tb=tb,
                                   _bank=bank, _h=h: t.matmul(
                            _bank[:, 0:TB],
                            wT[_ws][:, _cc, :],
                            xT[_h][:, _cc, _tb * TB:(_tb + 1) * TB],
                            start=(_cc == 0), stop=(_cc == CC - 1)),
                            waits=w, inc=(c_pe if cc == CC - 1 else None))
                    qs = qk_i[0] % 4
                    qk_i[0] += 1
                    sl = slice(tb * TB, (tb + 1) * TB)
                    r1 = em.op("dve", lambda v, _b=bank, _sl=sl, _h=hr:
                               v.tensor_tensor(
                        rot[0:64, :], _b[64:128, 0:TB], srs[_h][0:64, _sl],
                        ALU.mult),
                        waits=[(c_pe, mm), rope_ready, qkst_free[qs],
                               (c_dve, c_dve.n)], inc=c_dve)
                    em.op("dve", lambda v, _b=bank, _sl=sl, _h=hr:
                          v.tensor_tensor(
                        rot[64:128, :], _b[0:64, 0:TB], srs[_h][64:128, _sl],
                        ALU.mult), inc=c_dve)
                    r3 = em.op("dve", lambda v, _b=bank, _sl=sl, _h=hr:
                               v.tensor_tensor(
                        qtmp[:, :], _b[:, 0:TB], src_[_h][:, _sl], ALU.mult),
                        inc=c_dve)
                    r4 = em.op("dve", lambda v, _qs=qs: v.tensor_tensor(
                        qkst[_qs][:, :], qtmp[:, :], rot[:, :], ALU.add),
                        waits=[(c_dve, r3)], inc=c_dve)
                    bank_qk_free[bk] = (c_dve, r4)
                    rope_last_use[hr] = (c_dve, r4)
                    mst = em.op("act", lambda s, _scr=scr, _d=dt, _o=TOFF,
                                _tb=tb, _qs=qs: s.dma_start(
                        out=_scr[_d, :, _o + _tb * TB:_o + (_tb + 1) * TB],
                        in_=qkst[_qs][:, :]),
                        waits=[(c_dve, r4)], inc=d_qk[qs], inc_k=16)
                    qkst_free[qs] = (d_qk[qs], mst)
                wT_free[ws] = (c_pe, c_pe.n)

        for ds in range(NVDS):
            wvs = 0
            M_wv = []
            for j in range(VDS // 128):
                s, rdy = stage_load(
                    (lambda _ds=ds, _j=j:
                     wv[_ds * VDS + _j * 128:_ds * VDS + (_j + 1) * 128, :]))
                mwv = em.op("sp", lambda e, _s=s, _j=j, _wvs=wvs: e.dma_start(
                    out=wvT[_wvs][:, :, _j * 128:(_j + 1) * 128],
                    in_=stage[_s][:, :], transpose=True),
                    waits=[rdy] + ([wvT_free[wvs]] if j == 0 else []),
                    inc=d_tp[s], inc_k=16)
                stage_free[s] = (d_tp[s], mwv)
                M_wv.append((d_tp[s], mwv))
            last_pe_v = None
            for tt in range(TCH // 128):
                bk = tt % 2
                bank = [pC, pD][bk]
                mm = None
                for cc in range(CC):
                    w = (M_wv + tb_ready((tt * 128) // TB)
                         + [bank_v_free[bk]]) if cc == 0 else []
                    mm = em.op("pe", lambda t, _cc=cc, _tt=tt, _bank=bank,
                               _h=h, _wvs=wvs: t.matmul(
                        _bank[:, 0:VDS],
                        xT[_h][:, _cc, _tt * 128:(_tt + 1) * 128],
                        wvT[_wvs][:, _cc, :],
                        start=(_cc == 0), stop=(_cc == CC - 1)),
                        waits=w, inc=(c_pe if cc == CC - 1 else None))
                last_pe_v = mm
                vsl = vs_i[0] % 2
                vs_i[0] += 1
                me = em.op("dve", lambda v, _b=bank, _vs=vsl: v.tensor_copy(
                    vst[_vs][:, 0:VDS], _b[:, 0:VDS]),
                    waits=[(c_pe, mm), vst_free[vsl]], inc=c_dve)
                bank_v_free[bk] = (c_dve, me)
                lastst = None
                for hh in range(VDS // 128):
                    hd = (ds * VDS) // 128 + hh
                    lastst = em.op("act", lambda s, _hd=hd, _o=TOFF, _tt=tt,
                                   _vs=vsl, _hh=hh: s.dma_start(
                        out=v_scr[_hd, _o + _tt * 128:_o + (_tt + 1) * 128, :],
                        in_=vst[_vs][:, _hh * 128:(_hh + 1) * 128]),
                        waits=[(c_dve, me)], inc=d_vs[vsl], inc_k=16)
                vst_free[vsl] = (d_vs[vsl], lastst)
            wvT_free[wvs] = (c_pe, last_pe_v)
        xT_free[h] = (c_pe, c_pe.n)
        if ch + 1 < NCH:
            xT_marks = []
            build_xT_part(ch + 1, 0, min(TB // 128, NTR), xT_marks)

    em.barrier()
    free_phase()

    NHS = 3
    qh = [sbf("qh%d" % i, [128, T], BF16) for i in range(NHS)]
    kh = [sbf("kh%d" % i, [128, T], BF16) for i in range(NHS)]
    vh = [sbf("vh%d" % i, [128, T // 128, 128], BF16) for i in range(NHS)]
    pT = [sbf("pT%d" % i, [128, QG], BF16) for i in range(6)]
    r_row = sbf("r_row", [1, QG], F32)
    brd = sbf("brd", [128, QG], F32)
    aost = [sbf("aost0", [128, QG], BF16), sbf("aost1", [128, QG], BF16)]
    aost_free = [(d_as[0], 0), (d_as[1], 0)]
    as_i = [0]

    head_slot_free = [(c_pe, 0), (c_pe, 0), (c_pe, 0)]
    head_ready = [None, None, None]
    sbank_free = {i: (c_act, 0) for i in range(4)}
    pT_free = [(c_pe, 0)] * 6
    pF_free = (c_dve, 0)
    pE_free = (c_dve, 0)
    pD_free = (c_dve, 0)

    def load_head(hd):
        hs = hd % NHS
        w = [head_slot_free[hs]]
        mq = em.op("act", lambda s, _h=hd, _hs=hs: s.dma_start(
            out=qh[_hs][:, :], in_=q_scr[_h, :, :]), waits=w,
            inc=d_qh[hs], inc_k=16)
        mk = em.op("act", lambda s, _h=hd, _hs=hs: s.dma_start(
            out=kh[_hs][:, :], in_=k_scr[_h, :, :]), inc=d_kh[hs], inc_k=16)
        mv = em.op("act", lambda s, _h=hd, _hs=hs: s.dma_start(
            out=vh[_hs][:, :, :],
            in_=v_scr[_h].rearrange("(sc p) h -> p sc h", p=128)),
            inc=d_vh[hs], inc_k=16)
        head_ready[hs] = [(d_qh[hs], mq), (d_kh[hs], mk), (d_vh[hs], mv)]

    load_head(0)
    if HPC > 1:
        load_head(1)
    if HPC > 2:
        load_head(2)

    pstg = [sbf("pstg%d" % i, [128, C], BF16) for i in range(3)]
    ps_free = [(d_pw[i], 0) for i in range(3)]
    psl = [0]
    pre_marks = {}
    PLOOK = 2

    def emit_prestage(w_src, dstT):
        pre_state = {}
        for i in range(CC + PLOOK):
            if i < CC:
                dt = i
                s = psl[0] % 3
                psl[0] += 1
                ml = em.op("pool", lambda g, _w=w_src, _d=dt, _s=s:
                           g.dma_start(
                    out=pstg[_s][:, :], in_=_w[_d * 128:(_d + 1) * 128, :]),
                    waits=[ps_free[s]], inc=d_ps[s], inc_k=16)
                pre_state[i] = (ml, s)
            j = i - PLOOK
            if j >= 0:
                dt = j
                ml, s = pre_state.pop(j)
                mw = em.op("pool", lambda g, _dst=dstT, _d=dt, _s=s:
                           g.dma_start(
                    out=_dst[_d * 128:(_d + 1) * 128, :],
                    in_=pstg[_s][:, :]),
                    waits=[(d_ps[s], ml)], inc=d_pw[s], inc_k=16)
                ps_free[s] = (d_pw[s], mw)
                pre_marks[(id(dstT), dt)] = (d_pw[s], mw)

    emit_prestage(wo, woT_d)

    NPT = len(pT)
    O_BANKS = [pE, pG]
    S_BANKS = [pF, pH]
    o_free = [(c_dve, 0), (c_dve, 0)]
    s_free = [(c_dve, 0), (c_dve, 0)]
    brd2 = [brd, sbf("brd1", [128, QG], F32)]
    rrow2 = [r_row, sbf("r_row1", [1, QG], F32)]
    brd_free = [(c_dve, 0), (c_dve, 0)]
    rrow_free = [(c_pe, 0), (c_pe, 0)]
    pending_ep = []
    gq_i = [0]

    def flush_ep():
        while pending_ep:
            pending_ep.pop(0)()

    M_ast_half = [None]
    LOOK = 4
    tiles = []
    for hd in range(HPC):
        for qg in range(NQG):
            nst = qg * STG + STG
            for st in range(nst):
                tiles.append((hd, qg, st, nst))
    NT2 = len(tiles)
    M_exp_map = {}
    M_sums = None
    heads_first_tile = set()
    seen_heads = set()
    for gi, (hd, qg, st, nst) in enumerate(tiles):
        if hd not in seen_heads:
            seen_heads.add(hd)
            heads_first_tile.add(gi)

    for i in range(NT2 + LOOK):
        if i < NT2:
            hd, qg, st, nst = tiles[i]
            hs = hd % NHS
            gslot = (hd * NQG + qg) % 2
            pk = st - (nst - STG)
            v0 = 0 if pk < 0 else pk * 128
            bk = i % 4
            sbank = [pA, pB, pC, pD][bk]
            wsc = [sbank_free[bk]]
            if i in heads_first_tile:
                wsc += (head_ready[hs] or [])
            msc = em.op("pe", lambda t, _hs=hs, _st=st, _qg=qg, _v0=v0,
                        _b=sbank: t.matmul(
                _b[:, _v0:QG],
                kh[_hs][:, _st * 128:(_st + 1) * 128],
                qh[_hs][:, _qg * QG + _v0:(_qg + 1) * QG],
                start=True, stop=True),
                waits=wsc, inc=c_pe)
            if st == 2:
                flush_ep()
            wexp = [(c_pe, msc), pT_free[i % NPT]]
            if pk >= 0:
                mmask = em.op("dve", lambda v, _b=sbank, _v0=v0:
                              v.tensor_tensor(
                    _b[:, _v0:_v0 + 128], _b[:, _v0:_v0 + 128],
                    mask_tri[:, :], ALU.add),
                    waits=[(c_pe, msc), (c_pool, M_consts),
                           (c_dve, c_dve.n)], inc=c_dve)
                wexp.append((c_dve, mmask))
            mex = em.op("act", lambda s, _b=sbank, _v0=v0, _i=i:
                        s.activation(
                pT[_i % NPT][:, _v0:QG], _b[:, _v0:QG], AF.Exp,
                bias=kconst[:, 0:1], scale=SC),
                waits=wexp, inc=c_act)
            sbank_free[bk] = (c_act, mex)
            M_exp_map[i] = mex
        j = i - LOOK
        if j < 0:
            continue
        hd, qg, st, nst = tiles[j]
        hs = hd % NHS
        gslot = (hd * NQG + qg) % 2
        if qg == 0 and st == 0 and hd >= 1 and hd + 2 < HPC:
            load_head(hd + 2)
        pk = st - (nst - STG)
        v0 = 0 if pk < 0 else pk * 128
        obank = O_BANKS[gslot]
        sbank_sum = S_BANKS[gslot]
        wsum = [(c_act, M_exp_map.pop(j))]
        if st == 0:
            wsum.append(s_free[gslot])
        ms_ = em.op("pe", lambda t, _st=st, _v0=v0, _n=nst, _j=j,
                    _sb=sbank_sum: t.matmul(
            _sb[0:1, _v0:QG], ones_cb[:, 0:1],
            pT[_j % NPT][:, _v0:QG],
            start=(_st == 0), stop=(_st == _n - 1)),
            waits=wsum, inc=(c_pe if st == nst - 1 else None))
        if st == nst - 1:
            M_sums = ms_
        wpv = [o_free[gslot]] if st == 0 else []
        mpv = em.op("pe", lambda t, _hs=hs, _st=st, _v0=v0, _n=nst, _j=j,
                    _ob=obank: t.matmul(
            _ob[:, _v0:QG], vh[_hs][:, _st, :],
            pT[_j % NPT][:, _v0:QG],
            start=(_st == 0), stop=(_st == _n - 1)),
            waits=wpv, inc=c_pe)
        pT_free[j % NPT] = (c_pe, mpv)
        if st == nst - 1:
            M_pv = mpv
            mrec = em.op("dve", lambda v, _sb=sbank_sum, _g=gslot:
                         v.reciprocal(
                rrow2[_g][0:1, 0:QG], _sb[0:1, 0:QG]),
                waits=[(c_pe, M_sums), rrow_free[gslot],
                       (c_dve, c_dve.n)], inc=c_dve)
            s_free[gslot] = (c_dve, mrec)

            def make_ep(_hd=hd, _qg=qg, _g=gslot, _ob=obank, _mrec=mrec,
                        _M_pv=M_pv, _sb=sbank_sum):
                def ep():
                    mbc = em.op("pe", lambda t: t.matmul(
                        _sb[:, 0:QG],
                        ones_rf[0:1, :], rrow2[_g][0:1, 0:QG],
                        start=True, stop=True),
                        waits=[(c_dve, _mrec), (c_pool, M_consts)], inc=c_pe)
                    rrow_free[_g] = (c_pe, mbc)
                    mbcc = em.op("dve", lambda v: v.tensor_copy(
                        brd2[_g][:, 0:QG], _sb[:, 0:QG]),
                        waits=[(c_pe, mbc), brd_free[_g]], inc=c_dve)
                    s_free[_g] = (c_dve, mbcc)
                    asl = as_i[0] % 2
                    as_i[0] += 1
                    mnrm = em.op("dve", lambda v, _a=asl: v.tensor_tensor(
                        aost[_a][:, :], _ob[:, 0:QG], brd2[_g][:, 0:QG],
                        ALU.mult),
                        waits=[(c_dve, mbcc), (c_pe, _M_pv), aost_free[asl]],
                        inc=c_dve)
                    o_free[_g] = (c_dve, mnrm)
                    brd_free[_g] = (c_dve, mnrm)
                    mast = em.op("act", lambda s, _a=asl: s.dma_start(
                        out=a_in[_hd * 128:(_hd + 1) * 128,
                                 _qg * QG:(_qg + 1) * QG],
                        in_=aost[_a][:, :]),
                        waits=[(c_dve, mnrm)], inc=d_as[asl], inc_k=16)
                    aost_free[asl] = (d_as[asl], mast)
                    if (_hd == HPC // 2 - 1 and _qg == NQG - 1
                            and HPC >= 2):
                        M_ast_half[0] = [(d_as[0], d_as[0].n),
                                         (d_as[1], d_as[1].n)]
                return ep
            pending_ep.append(make_ep())
            if qg == NQG - 1:
                head_slot_free[hs] = (c_pe, mpv)
                if hd == 0 and hd + 2 < HPC:
                    load_head(hd + 2)
    flush_ep()
    pE_free = o_free[0]

    M_ast = [(d_as[0], d_as[0].n), (d_as[1], d_as[1].n)]
    groups = [[0, 1, 2, 3], [4, 5, 6, 7]]
    if HPC >= 2:
        em.op("pool", lambda g: g.collective_compute(
            "AllGather", ALU.bypass, replica_groups=groups,
            ins=[a_in[0:Dc2, :].opt()], outs=[ag_outs[0].ap().opt()]),
            waits=M_ast_half[0], inc=c_cc)
        emit_prestage(fw, fwT_d)
        em.op("pool", lambda g: g.collective_compute(
            "AllGather", ALU.bypass, replica_groups=groups,
            ins=[a_in[Dc2:, :].opt()], outs=[ag_outs[1].ap().opt()]),
            waits=M_ast, inc=c_cc2)
    else:
        emit_prestage(fw, fwT_d)
        em.op("pool", lambda g: g.collective_compute(
            "AllGather", ALU.bypass, replica_groups=groups,
            ins=[a_in.ap().opt()], outs=[ag_outs[0].ap().opt()]),
            waits=M_ast, inc=c_cc)

    em.barrier(exclude=(c_cc,))
    free_phase()

    aT = sbf("aT", [128, CC, TH], BF16)
    x1 = sbf("x1", [128, TTH, C], F32)
    DTL = DB3 // 128
    w3T = [sbf("w3T0", [128, CC, DB3], BF16, is_transpose_dest=True),
           sbf("w3T1", [128, CC, DB3], BF16, is_transpose_dest=True)]
    w3stage = sbf("w3stage", [128, C], BF16)
    w3s_free = [(d_w3t[0], 0)]
    x2T = aT
    x2bf = [sbf("x2bfa", [128, C], BF16)]
    bias_b = sbf("bias_b", [128, C], F32)
    stt_t = sbf("stt", [128, 8, 8], F32)
    sfb = sbf("sfb", [1, C], F16)
    em.op("pool", lambda g: g.dma_start(out=sfb[:, :], in_=fb[:, :]),
          inc=d_fb, inc_k=16)

    pE_free3 = pE_free
    last_bb = None
    WW = min(512, C)
    for j in range(C // WW):
        mb = em.op("pe", lambda t, _j=j: t.matmul(
            pE[:, 0:WW], ones_rh[0:1, :], sfb[0:1, _j * WW:(_j + 1) * WW],
            start=True, stop=True),
            waits=[(d_fb, 16), pE_free3, (c_pool, M_consts)], inc=c_pe)
        last_bb = em.op("dve", lambda v, _j=j: v.tensor_copy(
            bias_b[:, _j * WW:(_j + 1) * WW], pE[:, 0:WW]),
            waits=[(c_pe, mb)], inc=c_dve)
        pE_free3 = (c_dve, last_bb)
    M_bias = (c_dve, last_bb)

    BANKS3 = [pA, pB, pC, pD, pE, pF, pG, pH]
    w3T_free = [(c_pe, 0), (c_pe, 0)]
    aT_free = (c_cc, 0)
    x1_free = (d_out, 0)
    x2bf_free = [(d_x2t, 0)]
    reg_free = {(tt, par): (c_dve, last_bb)
                for tt in range(TTH) for par in range(2)}
    state = {"M_xl": 0, "x2T_ready": {}}

    def sweep(hf, src_w, srcT, add_bias, first_sweep, at_marks=None):
        if first_sweep and NCCL == 2:
            cc_order = ([cc for cc in range(CC) if (cc % (2 * CCL2)) < CCL2]
                        + [cc for cc in range(CC) if (cc % (2 * CCL2)) >= CCL2])
        else:
            cc_order = list(range(CC))
        marks = {}
        for db in range(NDB):
            ws = db % 2
            M_w = []
            for j in range(DTL):
                dt = db * DTL + j
                ml = em.op("sp", lambda e, _dt=dt, _w=src_w: e.dma_start(
                    out=w3stage[:, :], in_=_w[_dt * 128:(_dt + 1) * 128, :]),
                    waits=[w3s_free[0], pre_marks[(id(src_w), dt)]],
                    inc=d_w3s[0], inc_k=16)
                mt = em.op("sp", lambda e, _j=j, _ws=ws: e.dma_start(
                    out=w3T[_ws][:, :, _j * 128:(_j + 1) * 128],
                    in_=w3stage[:, :], transpose=True),
                    waits=[(d_w3s[0], ml)]
                    + ([w3T_free[ws]] if j == 0 else []),
                    inc=d_w3t[0], inc_k=16)
                w3s_free[0] = (d_w3t[0], mt)
                M_w.append((d_w3t[0], mt))
            par = db % 2
            mm_tt = {}
            for tt in range(TTH):
                for ci, cc in enumerate(cc_order):
                    bank = BANKS3[tt * 2 + par]
                    w = []
                    if ci == 0:
                        w = M_w + [reg_free[(tt, par)]]
                        if first_sweep:
                            w.append(at_marks[0])
                        else:
                            w.append(state["x2T_ready"][tt])
                    if first_sweep and NCCL == 2 and ci == CC // 2:
                        w = w + [at_marks[1]]
                    mm = em.op("pe", lambda t, _cc=cc, _tt=tt, _ws=ws,
                               _ci=ci, _bank=bank, _sT=srcT: t.matmul(
                        _bank[:, 0:DB3],
                        _sT[:, _cc, _tt * 128:(_tt + 1) * 128],
                        w3T[_ws][:, _cc, :],
                        start=(_ci == 0), stop=(_ci == CC - 1)),
                        waits=w, inc=(c_pe if ci == CC - 1 else None))
                    if ci == CC - 1:
                        mm_tt[tt] = mm
            w3T_free[ws] = (c_pe, mm_tt[TTH - 1])
            for tt in range(TTH):
                bank = BANKS3[tt * 2 + par]
                me = em.op("dve", lambda v, _tt=tt, _db=db, _bank=bank,
                           _par=par: v.tensor_tensor(
                    x1[:, _tt, _db * DB3:(_db + 1) * DB3],
                    _bank[:, 0:DB3],
                    x1[:, _tt, _db * DB3:(_db + 1) * DB3], ALU.add),
                    waits=[(c_pe, mm_tt[tt]), (d_xl, state["M_xl"]),
                           (c_dve, c_dve.n)], inc=c_dve)
                if add_bias:
                    me = em.op("dve", lambda v, _tt=tt, _db=db: v.tensor_tensor(
                        x1[:, _tt, _db * DB3:(_db + 1) * DB3],
                        x1[:, _tt, _db * DB3:(_db + 1) * DB3],
                        bias_b[:, _db * DB3:(_db + 1) * DB3], ALU.add),
                        waits=[(c_dve, me), M_bias], inc=c_dve)
                reg_free[(tt, par)] = (c_dve, me)
                marks[tt] = me
        return marks

    def layernorm(tt, evac_mark, li):
        stt = stt_t[:, (li % 2) * 4 + tt, :]
        xs = x2bf[0]
        mss = em.op("dve", lambda v, _tt=tt, _s=stt: v.reduce_sum(
            _s[:, 0:1], x1[:, _tt, :], axis=AX.X),
            waits=[(c_dve, evac_mark)], inc=c_dve)
        msq = em.op("act", lambda s, _tt=tt, _s=stt, _xs=xs: s.activation(
            _xs[:, :], x1[:, _tt, :], AF.Square, accum_out=_s[:, 2:3]),
            waits=[(c_dve, evac_mark), x2bf_free[0]], inc=c_act)
        mm_ = em.op("dve", lambda v, _s=stt: v.tensor_scalar_mul(
            _s[:, 1:2], _s[:, 0:1], kconst[:, 2:3]), waits=[(c_dve, mss)],
            inc=c_dve)
        me2 = em.op("dve", lambda v, _s=stt: v.tensor_scalar_mul(
            _s[:, 3:4], _s[:, 2:3], kconst[:, 2:3]), waits=[(c_act, msq)],
            inc=c_dve)
        mq2 = em.op("dve", lambda v, _s=stt: v.tensor_tensor(
            _s[:, 4:5], _s[:, 1:2], _s[:, 1:2], ALU.mult),
            waits=[(c_dve, mm_)], inc=c_dve)
        mvar = em.op("dve", lambda v, _s=stt: v.tensor_tensor(
            _s[:, 5:6], _s[:, 3:4], _s[:, 4:5], ALU.subtract),
            waits=[(c_dve, mq2), (c_dve, me2)], inc=c_dve)
        msd = em.op("act", lambda s, _s=stt: s.activation(
            _s[:, 6:7], _s[:, 5:6], AF.Sqrt, bias=kconst[:, 1:2]),
            waits=[(c_dve, mvar)], inc=c_act)
        mrs = em.op("dve", lambda v, _s=stt: v.reciprocal(
            _s[:, 7:8], _s[:, 6:7]), waits=[(c_act, msd)], inc=c_dve)
        mapp = em.op("dve", lambda v, _tt=tt, _s=stt: v.tensor_scalar(
            x1[:, _tt, :], x1[:, _tt, :], _s[:, 1:2], _s[:, 7:8],
            ALU.subtract, ALU.mult), waits=[(c_dve, mrs)], inc=c_dve)
        return mapp

    NCCL = 2 if HPC >= 2 else 1
    CCL2 = Dc2 // 128
    for hf in range(NHF):
        toff = hf * TH
        at_marks = []
        for part in range(NCCL):
            ccw = (c_cc, 1) if part == 0 else (
                (c_cc2, 1) if NCCL == 2 else (c_cc, 1))
            em.wait("act", [ccw] + ([aT_free] if part == 0 else []))
            nldm = 1 if NCCL == 1 else CCL2
            d_at[part].n += 16 * nldm
            at_marks.append((d_at[part], d_at[part].n))

            def _run_at(s, _hf=hf, _sem=d_at[part], _part=part, _n=nldm):
                for arm in nc.switch_core_id(engines=[s], n=n_arms):
                    g = (arm.logical % N_CORES) % CPB
                    off = g * TPC + _hf * TH
                    if NCCL == 1:
                        s.dma_start(
                            out=aT[:, :, :],
                            in_=ag_outs[0].ap().rearrange(
                                "(cc p) t -> p cc t", p=128)
                            [:, :, off:off + TH]).then_inc(_sem.h, 16)
                    else:
                        for cq in range(_n):
                            s.dma_start(
                                out=aT.ap().rearrange(
                                    "p (r ccl) t -> p r ccl t", ccl=2 * CCL2)
                                [:, :, _part * CCL2 + cq, :],
                                in_=ag_outs[_part].ap().rearrange(
                                    "(rr cq p) t -> p rr cq t",
                                    p=128, cq=CCL2)
                                [:, :, cq, off:off + TH]).then_inc(_sem.h, 16)
            em.raw("act", _run_at)

        M_xl = None
        for tt in range(TTH):
            M_xl = em.op("act", lambda s, _tt=tt, _to=toff: s.dma_start(
                out=x1[:, _tt, :],
                in_=x_slice[_to + _tt * 128:_to + (_tt + 1) * 128, :]),
                waits=([x1_free] if tt == 0 else []), inc=d_xl, inc_k=16)
        state["M_xl"] = M_xl

        ev1 = sweep(hf, woT_d, aT, False, True, at_marks)
        sweep1_done = (c_pe, c_pe.n)
        state["x2T_ready"] = {}
        for tt in range(TTH):
            mapp = layernorm(tt, ev1[tt], hf * 2)
            mcast = em.op("act", lambda s, _tt=tt, _xs=x2bf[0]: s.copy(
                _xs[:, :], x1[:, _tt, :]),
                waits=[(c_dve, mapp), x2bf_free[0]], inc=c_act)
            mx2t = em.op("sp", lambda e, _tt=tt, _xs=x2bf[0]: e.dma_start(
                out=x2T[:, :, _tt * 128:(_tt + 1) * 128], in_=_xs[:, :],
                transpose=True),
                waits=[(c_act, mcast), sweep1_done],
                inc=d_x2t, inc_k=16)
            x2bf_free[0] = (d_x2t, mx2t)
            state["x2T_ready"][tt] = (d_x2t, mx2t)
        ev2 = sweep(hf, fwT_d, x2T, True, False)
        aT_free = (c_pe, c_pe.n)
        last_out = None
        for tt in range(TTH):
            mapp2 = layernorm(tt, ev2[tt], hf * 2 + 1)
            last_out = em.op("act", lambda s, _tt=tt, _to=toff: s.dma_start(
                out=out[_to + _tt * 128:_to + (_tt + 1) * 128, :],
                in_=x1[:, _tt, :]),
                waits=[(c_dve, mapp2)], inc=d_out, inc_k=16)
        x1_free = (d_out, last_out)

    if debug_taps:
        dbg = {}
        for nm, src_t, shp in (("dbg_q", q_scr, [HPC * 128, T]),
                               ("dbg_k", k_scr, [HPC * 128, T]),
                               ("dbg_v", v_scr, [HPC * T, 128]),
                               ("dbg_a", a_in, [Dc, T]),
                               ("dbg_ag", ag_out, [C, T])):
            o_t = nc.dram_tensor(nm, shp, BF16, kind="ExternalOutput")
            em.op("pool", lambda g, _o=o_t, _s=src_t, _shp=shp: g.dma_start(
                out=_o.ap(), in_=_s.ap().rearrange(
                    "a b c -> (a b) c") if len(_s.shape) == 3 else _s.ap()),
                waits=[(d_out, d_out.n)], inc=d_fb, inc_k=16)
        em.wait("pool", [(d_fb, d_fb.n)])
    em.wait("act", [(d_out, d_out.n)])
    em.wait("pool", [(d_out, d_out.n)])

    for c in em.ctrs.values():
        c.h = alloc(nc.semaphore(c.name), persist_ctx)
    with nc.Block() as block:
        @block.tensor
        def _(t):
            for fn in em.q["pe"]:
                fn(t)

        @block.vector
        def _(v):
            for fn in em.q["dve"]:
                fn(v)

        @block.scalar
        def _(s):
            for fn in em.q["act"]:
                fn(s)

        @block.gpsimd
        def _(g):
            for fn in em.q["pool"]:
                fn(g)

        @block.sync
        def _(e):
            for fn in em.q["sp"]:
                fn(e)

    free_phase()
    while persist_ctx:
        persist_ctx.pop().__exit__(None, None, None)
    return nc


def _rope_tables(T):
    h = HDIM
    pos = np.arange(T, dtype=np.float32)
    thetas = (1.0 / (np.float32(10000.0) **
                     (np.arange(0, h, 2, dtype=np.float32) / np.float32(h))))
    pt = np.outer(pos, thetas.astype(np.float32)).astype(np.float32)
    pt = np.concatenate([pt, pt], axis=-1)
    sin_t = np.sin(pt).astype(np.float32).T.copy()
    cos_t = np.cos(pt).astype(np.float32).T.copy()
    sin_t[0:h // 2] = -sin_t[0:h // 2]
    return np.ascontiguousarray(sin_t), np.ascontiguousarray(cos_t)


def kernel(L0_attn_x, L0_attn_wq, L0_attn_wk, L0_attn_wv, L0_attn_wo,
           L0_ffn_weight, L0_ffn_bias):
    x = np.ascontiguousarray(np.asarray(L0_attn_x, dtype=np.float32))
    wq = np.ascontiguousarray(np.asarray(L0_attn_wq, dtype=np.float32))
    wk = np.ascontiguousarray(np.asarray(L0_attn_wk, dtype=np.float32))
    wv = np.ascontiguousarray(np.asarray(L0_attn_wv, dtype=np.float32))
    wo = np.ascontiguousarray(np.asarray(L0_attn_wo, dtype=np.float32))
    fw = np.ascontiguousarray(np.asarray(L0_ffn_weight, dtype=np.float16))
    fb = np.ascontiguousarray(
        np.asarray(L0_ffn_bias, dtype=np.float16).reshape(1, -1))
    Bsz, T, C = x.shape
    assert Bsz == B
    TPC = T // CPB
    Dc = C // CPB
    rsin, rcos = _rope_tables(T)

    nc = build_program(T, C)

    in_maps = []
    for c in range(N_CORES):
        b, g = c // CPB, c % CPB
        in_maps.append({
            "x_full": x[b],
            "x_slice": np.ascontiguousarray(x[b, g * TPC:(g + 1) * TPC]),
            "wq": np.ascontiguousarray(wq[g * Dc:(g + 1) * Dc]),
            "wk": np.ascontiguousarray(wk[g * Dc:(g + 1) * Dc]),
            "wv": np.ascontiguousarray(wv[g * Dc:(g + 1) * Dc]),
            "wo": wo,
            "fw": fw,
            "fb": fb,
            "rsin": rsin,
            "rcos": rcos,
        })
    res = run_bass_kernel_spmd(nc, in_maps, core_ids=list(range(N_CORES)))
    outp = np.empty((B, T, C), dtype=np.float32)
    for c in range(N_CORES):
        b, g = c // CPB, c % CPB
        outp[b, g * TPC:(g + 1) * TPC] = res.results[c]["out"]
    return outp


# version 24
# speedup vs baseline: 1.0832x; 1.0832x over previous
import math
import numpy as np
import concourse.bass as bass
import concourse.mybir as mybir
from concourse.bass_utils import run_bass_kernel_spmd

F32 = mybir.dt.float32
BF16 = mybir.dt.bfloat16
F16 = mybir.dt.float16
AF = mybir.ActivationFunctionType
ALU = mybir.AluOpType
AX = mybir.AxisListType

N_CORES = 8
CPB = 4
B = 2
HDIM = 128
SHIFT = 4.0
EPS = 1e-5


class Ctr:
    __slots__ = ("name", "n", "h")

    def __init__(self, name):
        self.name = name
        self.n = 0
        self.h = None


class Emit:
    ENGINES = ("pe", "dve", "act", "pool", "sp")

    def __init__(self):
        self.q = {e: [] for e in self.ENGINES}
        self.ctrs = {}

    def ctr(self, name):
        c = Ctr(name)
        self.ctrs[name] = c
        return c

    def op(self, eng, fn, waits=(), inc=None, inc_k=1):
        ws = tuple((c, v) for (c, v) in waits if v > 0)
        m = None
        if inc is not None:
            inc.n += inc_k
            m = inc.n

        def run(e, _fn=fn, _ws=ws, _inc=inc, _k=inc_k):
            for (c, v) in _ws:
                e.wait_ge(c.h, v)
            ins = _fn(e)
            if _inc is not None and ins is not None:
                ins.then_inc(_inc.h, _k)
        self.q[eng].append(run)
        return m

    def wait(self, eng, waits):
        ws = tuple((c, v) for (c, v) in waits if v > 0)
        if not ws:
            return

        def run(e, _ws=ws):
            for (c, v) in _ws:
                e.wait_ge(c.h, v)
        self.q[eng].append(run)

    def raw(self, eng, fn):
        self.q[eng].append(fn)

    def barrier(self, exclude=()):
        snap = [(c, c.n) for c in self.ctrs.values()
                if c.n > 0 and c not in exclude]
        for e in self.ENGINES:
            self.wait(e, snap)


def build_program(T, C, n_arms=128, debug_taps=False):
    NH = C // HDIM
    HPC = NH // CPB
    Dc = HPC * HDIM
    CC = C // 128
    TPC = T // CPB
    NCH = 2
    TCH = T // NCH
    TB = min(512, TCH)
    NTB = TCH // TB
    QG = 512
    NQG = T // QG
    STG = QG // 128
    VDS = min(256, Dc)
    NVDS = Dc // VDS
    NHF = 2
    TH = TPC // NHF
    TTH = TH // 128
    DB3 = min(512, C)
    NDB = C // DB3
    SC = 1.0 / math.sqrt(HDIM)
    assert TCH % 128 == 0 and TCH % TB == 0 and T % QG == 0 and Dc % VDS == 0
    assert TH % 128 == 0 and C % DB3 == 0 and DB3 % 128 == 0

    nc = bass.Bass("TRN2", target_bir_lowering=False, debug=False)

    x_full = nc.dram_tensor("x_full", [T, C], F32, kind="ExternalInput")
    x_slice = nc.dram_tensor("x_slice", [TPC, C], F32, kind="ExternalInput")
    wq = nc.dram_tensor("wq", [Dc, C], F32, kind="ExternalInput")
    wk = nc.dram_tensor("wk", [Dc, C], F32, kind="ExternalInput")
    wv = nc.dram_tensor("wv", [Dc, C], F32, kind="ExternalInput")
    wo = nc.dram_tensor("wo", [C, C], F32, kind="ExternalInput")
    fw = nc.dram_tensor("fw", [C, C], F16, kind="ExternalInput")
    fb = nc.dram_tensor("fb", [1, C], F16, kind="ExternalInput")
    rsin = nc.dram_tensor("rsin", [128, T], F32, kind="ExternalInput")
    rcos = nc.dram_tensor("rcos", [128, T], F32, kind="ExternalInput")
    out = nc.dram_tensor("out", [TPC, C], F32, kind="ExternalOutput")

    q_scr = nc.dram_tensor("q_scr", [HPC, 128, T], BF16)
    k_scr = nc.dram_tensor("k_scr", [HPC, 128, T], BF16)
    v_scr = nc.dram_tensor("v_scr", [HPC, T, 128], BF16)
    a_in = nc.dram_tensor("a_in", [Dc, T], BF16)
    Dc2 = Dc // 2
    ag_outs = [nc.dram_tensor("ag_out0", [CPB * Dc2, T], BF16),
               nc.dram_tensor("ag_out1", [CPB * Dc2, T], BF16)]
    woT_d = nc.dram_tensor("woT_d", [C // 128, 128, CC, 128], BF16)
    fwT_d = nc.dram_tensor("fwT_d", [C // 128, 128, CC, 128], BF16)

    em = Emit()
    c_pe = em.ctr("c_pe"); c_dve = em.ctr("c_dve"); c_act = em.ctr("c_act")
    c_pool = em.ctr("c_pool"); c_cc = em.ctr("c_cc")
    d_stg = [em.ctr("d_stg0"), em.ctr("d_stg1")]
    d_tp = [em.ctr("d_tp0"), em.ctr("d_tp1")]
    d_xtp = [em.ctr("d_xtp0"), em.ctr("d_xtp1")]
    d_rope = em.ctr("d_rope")
    d_qk = [em.ctr("d_qk%d" % i) for i in range(4)]
    d_vs = [em.ctr("d_vs0"), em.ctr("d_vs1")]
    d_qh = [em.ctr("d_qh%d" % i) for i in range(3)]
    d_kh = [em.ctr("d_kh%d" % i) for i in range(3)]
    d_vh = [em.ctr("d_vh%d" % i) for i in range(3)]
    d_as = [em.ctr("d_as%d" % i) for i in range(4)]
    d_at = [em.ctr("d_at0"), em.ctr("d_at1")]
    d_xl = em.ctr("d_xl")
    d_w3t = [em.ctr("d_w3t0"), em.ctr("d_w3t1")]
    d_ps = [em.ctr("d_ps%d" % i) for i in range(3)]
    d_ptp = [em.ctr("d_ptp%d" % i) for i in range(3)]
    d_pw = [em.ctr("d_pw%d" % i) for i in range(3)]
    d_x2t = em.ctr("d_x2t")
    d_fb = em.ctr("d_fb")
    d_out = em.ctr("d_out")

    persist_ctx = []
    phase_ctx = []

    def alloc(cm, ctx_list):
        g = cm.__enter__()
        ctx_list.append(cm)
        return g

    def sbp(name, shape, dt, **kw):
        return alloc(nc.sbuf_tensor(name, shape, dt, **kw), persist_ctx)

    def sbf(name, shape, dt, **kw):
        return alloc(nc.sbuf_tensor(name, shape, dt, **kw), phase_ctx)

    def free_phase():
        while phase_ctx:
            phase_ctx.pop().__exit__(None, None, None)

    mask_tri = sbp("mask_tri", [128, 128], F32)
    kconst = sbp("kconst", [128, 3], F32)
    ones_cb = sbp("ones_cb", [128, 1], BF16)
    ones_rf = sbp("ones_rf", [1, 128], F32)
    ones_rh = sbp("ones_rh", [1, 128], F16)
    pA = alloc(nc.psum_tensor("pA", [128, 512], F32), persist_ctx)
    pB = alloc(nc.psum_tensor("pB", [128, 512], F32), persist_ctx)
    pC = alloc(nc.psum_tensor("pC", [128, 512], F32), persist_ctx)
    pD = alloc(nc.psum_tensor("pD", [128, 512], F32), persist_ctx)
    pE = alloc(nc.psum_tensor("pE", [128, 512], F32), persist_ctx)
    pF = alloc(nc.psum_tensor("pF", [128, 512], F32), persist_ctx)
    pG = alloc(nc.psum_tensor("pG", [128, 512], F32), persist_ctx)
    pH = alloc(nc.psum_tensor("pH", [128, 512], F32), persist_ctx)

    m0 = em.op("pool", lambda g: g.memset(mask_tri[:, :], 0.0), inc=c_pool)
    em.op("pool", lambda g: g.affine_select(
        out=mask_tri[:, :], in_=mask_tri[:, :], compare_op=ALU.is_ge,
        fill=-1e9, base=0, pattern=[[1, 128]], channel_multiplier=-1),
        waits=[(c_pool, m0)], inc=c_pool)
    em.op("pool", lambda g: g.memset(ones_cb[:, :], 1.0), inc=c_pool)
    em.op("pool", lambda g: g.memset(ones_rf[:, :], 1.0), inc=c_pool)
    em.op("pool", lambda g: g.memset(ones_rh[:, :], 1.0), inc=c_pool)
    em.op("pool", lambda g: g.memset(kconst[:, 0:1], -SHIFT), inc=c_pool)
    em.op("pool", lambda g: g.memset(kconst[:, 1:2], EPS), inc=c_pool)
    em.op("pool", lambda g: g.memset(kconst[:, 2:3], 1.0 / C), inc=c_pool)
    M_consts = c_pool.n

    stage = [sbf("stage0", [128, C], BF16), sbf("stage1", [128, C], BF16)]
    stage_free = [(c_pool, 0), (c_pool, 0)]
    stage_i = [0]

    def stage_load(src_fn):
        s = stage_i[0] % 2
        stage_i[0] += 1
        mset = em.op("pool", lambda g, _f=src_fn, _s=s: g.dma_start(
            out=stage[_s][:, :], in_=_f()), waits=[stage_free[s]],
            inc=d_stg[s], inc_k=16)
        return s, (d_stg[s], mset)

    xT = [sbf("xT0", [128, CC, TCH], BF16, is_transpose_dest=True)]
    srs = [sbf("srs0", [128, TCH], F32)]
    src_ = [sbf("src0", [128, TCH], F32)]
    wT = [sbf("wT0", [128, CC, 128], BF16, is_transpose_dest=True),
          sbf("wT1", [128, CC, 128], BF16, is_transpose_dest=True)]
    wT_free = [(c_pe, 0), (c_pe, 0)]
    wvT = [sbf("wvT0", [128, CC, VDS], BF16, is_transpose_dest=True)]
    wvT_free = [(c_pe, 0)]
    rot = sbf("rot", [128, TB], F32)
    qtmp = sbf("qtmp", [128, TB], F32)
    qkst = [sbf("qkst%d" % i, [128, TB], BF16) for i in range(4)]
    qkst_free = [(d_qk[i], 0) for i in range(4)]
    qk_i = [0]
    vst = [sbf("vst0", [128, VDS], BF16), sbf("vst1", [128, VDS], BF16)]
    vst_free = [(d_vs[0], 0), (d_vs[1], 0)]
    vs_i = [0]

    rope_last_use = [(c_dve, 0)]
    xT_free = [(c_pe, 0), (c_pe, 0)]
    QK_BANKS = [pA, pB, pE, pF]
    bank_qk_free = [(c_dve, 0)] * 4
    bank_v_free = [(c_dve, 0), (c_dve, 0)]
    qkb_i = [0]

    def build_xT_part(ch, tr0, tr1, marks):
        h = 0
        TOFF = ch * TCH
        for tr in range(tr0, tr1):
            s, rdy = stage_load(
                (lambda _o=TOFF + tr * 128: x_full[_o:_o + 128, :]))
            mxt = em.op("sp", lambda e, _s=s, _tr=tr, _h=h: e.dma_start(
                out=xT[_h][:, :, _tr * 128:(_tr + 1) * 128],
                in_=stage[_s][:, :], transpose=True),
                waits=[rdy] + ([xT_free[h]] if tr == 0 else []),
                inc=d_xtp[s], inc_k=16)
            stage_free[s] = (d_xtp[s], mxt)
            marks.append((d_xtp[s], mxt))

    NTR = TCH // 128
    xT_marks = []
    build_xT_part(0, 0, min(TB // 128, NTR), xT_marks)

    for ch in range(NCH):
        h = 0
        hr = 0
        TOFF = ch * TCH
        mr1 = em.op("act", lambda s, _o=TOFF, _h=hr: s.dma_start(
            out=srs[_h][:, :], in_=rsin[:, _o:_o + TCH]),
            waits=[rope_last_use[hr]], inc=d_rope, inc_k=16)
        mr2 = em.op("act", lambda s, _o=TOFF, _h=hr: s.dma_start(
            out=src_[_h][:, :], in_=rcos[:, _o:_o + TCH]),
            inc=d_rope, inc_k=16)
        rope_ready = (d_rope, mr2)
        if len(xT_marks) < NTR:
            build_xT_part(ch, len(xT_marks), NTR, xT_marks)

        def tb_ready(tb):
            return xT_marks[tb * (TB // 128):(tb + 1) * (TB // 128)]

        for widx, (wten, scr) in enumerate(((wq, q_scr), (wk, k_scr))):
            for dt in range(HPC):
                ws = (widx * HPC + dt) % 2
                s, rdy = stage_load(
                    (lambda _w=wten, _d=dt: _w[_d * 128:(_d + 1) * 128, :]))
                mwt = em.op("sp", lambda e, _s=s, _ws=ws: e.dma_start(
                    out=wT[_ws][:, :, :], in_=stage[_s][:, :], transpose=True),
                    waits=[rdy, wT_free[ws]], inc=d_tp[s], inc_k=16)
                stage_free[s] = (d_tp[s], mwt)
                wt_rdy = (d_tp[s], mwt)
                for tb in range(NTB):
                    bk = qkb_i[0] % 4
                    qkb_i[0] += 1
                    bank = QK_BANKS[bk]
                    mm = None
                    for cc in range(CC):
                        w = ([wt_rdy, bank_qk_free[bk]] + tb_ready(tb)
                             if cc == 0 else [])
                        mm = em.op("pe", lambda t, _ws=ws, _cc=cc, _tb=tb,
                                   _bank=bank, _h=h: t.matmul(
                            _bank[:, 0:TB],
                            wT[_ws][:, _cc, :],
                            xT[_h][:, _cc, _tb * TB:(_tb + 1) * TB],
                            start=(_cc == 0), stop=(_cc == CC - 1)),
                            waits=w, inc=(c_pe if cc == CC - 1 else None))
                    qs = qk_i[0] % 4
                    qk_i[0] += 1
                    sl = slice(tb * TB, (tb + 1) * TB)
                    r1 = em.op("dve", lambda v, _b=bank, _sl=sl, _h=hr:
                               v.tensor_tensor(
                        rot[0:64, :], _b[64:128, 0:TB], srs[_h][0:64, _sl],
                        ALU.mult),
                        waits=[(c_pe, mm), rope_ready, qkst_free[qs],
                               (c_dve, c_dve.n)], inc=c_dve)
                    em.op("dve", lambda v, _b=bank, _sl=sl, _h=hr:
                          v.tensor_tensor(
                        rot[64:128, :], _b[0:64, 0:TB], srs[_h][64:128, _sl],
                        ALU.mult), inc=c_dve)
                    r3 = em.op("dve", lambda v, _b=bank, _sl=sl, _h=hr:
                               v.tensor_tensor(
                        qtmp[:, :], _b[:, 0:TB], src_[_h][:, _sl], ALU.mult),
                        inc=c_dve)
                    r4 = em.op("dve", lambda v, _qs=qs: v.tensor_tensor(
                        qkst[_qs][:, :], qtmp[:, :], rot[:, :], ALU.add),
                        waits=[(c_dve, r3)], inc=c_dve)
                    bank_qk_free[bk] = (c_dve, r4)
                    rope_last_use[hr] = (c_dve, r4)
                    mst = em.op("act", lambda s, _scr=scr, _d=dt, _o=TOFF,
                                _tb=tb, _qs=qs: s.dma_start(
                        out=_scr[_d, :, _o + _tb * TB:_o + (_tb + 1) * TB],
                        in_=qkst[_qs][:, :]),
                        waits=[(c_dve, r4)], inc=d_qk[qs], inc_k=16)
                    qkst_free[qs] = (d_qk[qs], mst)
                wT_free[ws] = (c_pe, c_pe.n)

        for ds in range(NVDS):
            wvs = 0
            M_wv = []
            for j in range(VDS // 128):
                s, rdy = stage_load(
                    (lambda _ds=ds, _j=j:
                     wv[_ds * VDS + _j * 128:_ds * VDS + (_j + 1) * 128, :]))
                mwv = em.op("sp", lambda e, _s=s, _j=j, _wvs=wvs: e.dma_start(
                    out=wvT[_wvs][:, :, _j * 128:(_j + 1) * 128],
                    in_=stage[_s][:, :], transpose=True),
                    waits=[rdy] + ([wvT_free[wvs]] if j == 0 else []),
                    inc=d_tp[s], inc_k=16)
                stage_free[s] = (d_tp[s], mwv)
                M_wv.append((d_tp[s], mwv))
            last_pe_v = None
            for tt in range(TCH // 128):
                bk = tt % 2
                bank = [pC, pD][bk]
                mm = None
                for cc in range(CC):
                    w = (M_wv + tb_ready((tt * 128) // TB)
                         + [bank_v_free[bk]]) if cc == 0 else []
                    mm = em.op("pe", lambda t, _cc=cc, _tt=tt, _bank=bank,
                               _h=h, _wvs=wvs: t.matmul(
                        _bank[:, 0:VDS],
                        xT[_h][:, _cc, _tt * 128:(_tt + 1) * 128],
                        wvT[_wvs][:, _cc, :],
                        start=(_cc == 0), stop=(_cc == CC - 1)),
                        waits=w, inc=(c_pe if cc == CC - 1 else None))
                last_pe_v = mm
                vsl = vs_i[0] % 2
                vs_i[0] += 1
                me = em.op("dve", lambda v, _b=bank, _vs=vsl: v.tensor_copy(
                    vst[_vs][:, 0:VDS], _b[:, 0:VDS]),
                    waits=[(c_pe, mm), vst_free[vsl]], inc=c_dve)
                bank_v_free[bk] = (c_dve, me)
                lastst = None
                for hh in range(VDS // 128):
                    hd = (ds * VDS) // 128 + hh
                    lastst = em.op("act", lambda s, _hd=hd, _o=TOFF, _tt=tt,
                                   _vs=vsl, _hh=hh: s.dma_start(
                        out=v_scr[_hd, _o + _tt * 128:_o + (_tt + 1) * 128, :],
                        in_=vst[_vs][:, _hh * 128:(_hh + 1) * 128]),
                        waits=[(c_dve, me)], inc=d_vs[vsl], inc_k=16)
                vst_free[vsl] = (d_vs[vsl], lastst)
            wvT_free[wvs] = (c_pe, last_pe_v)
        xT_free[h] = (c_pe, c_pe.n)
        if ch + 1 < NCH:
            xT_marks = []
            build_xT_part(ch + 1, 0, min(TB // 128, NTR), xT_marks)

    em.barrier()
    free_phase()

    NHS = 3
    qh = [sbf("qh%d" % i, [128, T], BF16) for i in range(NHS)]
    kh = [sbf("kh%d" % i, [128, T], BF16) for i in range(NHS)]
    vh = [sbf("vh%d" % i, [128, T // 128, 128], BF16) for i in range(NHS)]
    pT = [sbf("pT%d" % i, [128, QG], BF16) for i in range(6)]
    r_row = sbf("r_row", [1, QG], F32)
    brd = sbf("brd", [128, QG], F32)
    aost = [sbf("aost%d" % i, [128, QG], BF16) for i in range(4)]
    aost_free = [(d_as[i], 0) for i in range(4)]
    as_i = [0]

    head_slot_free = [(c_pe, 0), (c_pe, 0), (c_pe, 0)]
    head_ready = [None, None, None]
    sbank_free = {i: (c_act, 0) for i in range(4)}
    pT_free = [(c_pe, 0)] * 6
    pF_free = (c_dve, 0)
    pE_free = (c_dve, 0)
    pD_free = (c_dve, 0)

    def load_head(hd):
        hs = hd % NHS
        w = [head_slot_free[hs]]
        mq = em.op("act", lambda s, _h=hd, _hs=hs: s.dma_start(
            out=qh[_hs][:, :], in_=q_scr[_h, :, :]), waits=w,
            inc=d_qh[hs], inc_k=16)
        mk = em.op("act", lambda s, _h=hd, _hs=hs: s.dma_start(
            out=kh[_hs][:, :], in_=k_scr[_h, :, :]), inc=d_kh[hs], inc_k=16)
        mv = em.op("act", lambda s, _h=hd, _hs=hs: s.dma_start(
            out=vh[_hs][:, :, :],
            in_=v_scr[_h].rearrange("(sc p) h -> p sc h", p=128)),
            inc=d_vh[hs], inc_k=16)
        head_ready[hs] = [(d_qh[hs], mq), (d_kh[hs], mk), (d_vh[hs], mv)]

    load_head(0)
    if HPC > 1:
        load_head(1)
    if HPC > 2:
        load_head(2)

    pstg = [sbf("pstg%d" % i, [128, C], BF16) for i in range(3)]
    ptp = [sbf("ptp%d" % i, [128, CC, 128], BF16, is_transpose_dest=True)
           for i in range(3)]
    ps_free = [(d_ptp[i], 0) for i in range(3)]
    ptp_free = [(d_pw[i], 0) for i in range(3)]
    psl = [0]
    pre_marks = {}
    PLOOK = 2

    def emit_prestage(w_src, dstT):
        pre_state = {}
        for i in range(CC + PLOOK):
            if i < CC:
                dt = i
                s = psl[0] % 3
                psl[0] += 1
                ml = em.op("pool", lambda g, _w=w_src, _d=dt, _s=s:
                           g.dma_start(
                    out=pstg[_s][:, :], in_=_w[_d * 128:(_d + 1) * 128, :]),
                    waits=[ps_free[s]], inc=d_ps[s], inc_k=16)
                mt = em.op("sp", lambda e, _s=s: e.dma_start(
                    out=ptp[_s][:, :, :], in_=pstg[_s][:, :], transpose=True),
                    waits=[(d_ps[s], ml), ptp_free[s]], inc=d_ptp[s],
                    inc_k=16)
                ps_free[s] = (d_ptp[s], mt)
                pre_state[i] = (mt, s)
            j = i - PLOOK
            if j >= 0:
                dt = j
                mt, s = pre_state.pop(j)
                mw = em.op("pool", lambda g, _dst=dstT, _d=dt, _s=s:
                           g.dma_start(
                    out=_dst[_d, :, :, :],
                    in_=ptp[_s][:, :, :]),
                    waits=[(d_ptp[s], mt)], inc=d_pw[s], inc_k=16)
                ptp_free[s] = (d_pw[s], mw)
                pre_marks[(id(dstT), dt)] = (d_pw[s], mw)

    emit_prestage(wo, woT_d)

    NPT = len(pT)
    O_BANKS = [pE, pG]
    S_BANKS = [pF, pH]
    o_free = [(c_dve, 0), (c_dve, 0)]
    s_free = [(c_dve, 0), (c_dve, 0)]
    brd2 = [brd, sbf("brd1", [128, QG], F32)]
    rrow2 = [r_row, sbf("r_row1", [1, QG], F32)]
    brd_free = [(c_dve, 0), (c_dve, 0)]
    rrow_free = [(c_pe, 0), (c_pe, 0)]
    pending_ep = []
    pending_st = []
    gq_i = [0]

    def flush_ep():
        old_st = list(pending_st)
        del pending_st[:]
        eps = list(pending_ep)
        del pending_ep[:]
        for f in old_st:
            f()
        for f in eps:
            f()

    M_ast_half = [None]
    LOOK = 4
    tiles = []
    for hd in range(HPC):
        for qg in range(NQG):
            nst = qg * STG + STG
            for st in range(nst):
                tiles.append((hd, qg, st, nst))
    NT2 = len(tiles)
    M_exp_map = {}
    M_sums = None
    heads_first_tile = set()
    seen_heads = set()
    for gi, (hd, qg, st, nst) in enumerate(tiles):
        if hd not in seen_heads:
            seen_heads.add(hd)
            heads_first_tile.add(gi)

    for i in range(NT2 + LOOK):
        if i < NT2:
            hd, qg, st, nst = tiles[i]
            hs = hd % NHS
            gslot = (hd * NQG + qg) % 2
            pk = st - (nst - STG)
            v0 = 0 if pk < 0 else pk * 128
            bk = i % 4
            sbank = [pA, pB, pC, pD][bk]
            wsc = [sbank_free[bk]]
            if i in heads_first_tile:
                wsc += (head_ready[hs] or [])
            msc = em.op("pe", lambda t, _hs=hs, _st=st, _qg=qg, _v0=v0,
                        _b=sbank: t.matmul(
                _b[:, _v0:QG],
                kh[_hs][:, _st * 128:(_st + 1) * 128],
                qh[_hs][:, _qg * QG + _v0:(_qg + 1) * QG],
                start=True, stop=True),
                waits=wsc, inc=c_pe)
            if st == 2:
                flush_ep()
            wexp = [(c_pe, msc), pT_free[i % NPT]]
            if pk >= 0:
                mmask = em.op("dve", lambda v, _b=sbank, _v0=v0:
                              v.tensor_tensor(
                    _b[:, _v0:_v0 + 128], _b[:, _v0:_v0 + 128],
                    mask_tri[:, :], ALU.add),
                    waits=[(c_pe, msc), (c_pool, M_consts),
                           (c_dve, c_dve.n)], inc=c_dve)
                wexp.append((c_dve, mmask))
            mex = em.op("act", lambda s, _b=sbank, _v0=v0, _i=i:
                        s.activation(
                pT[_i % NPT][:, _v0:QG], _b[:, _v0:QG], AF.Exp,
                bias=kconst[:, 0:1], scale=SC),
                waits=wexp, inc=c_act)
            sbank_free[bk] = (c_act, mex)
            M_exp_map[i] = mex
        j = i - LOOK
        if j < 0:
            continue
        hd, qg, st, nst = tiles[j]
        hs = hd % NHS
        gslot = (hd * NQG + qg) % 2
        if qg == 0 and st == 0 and hd >= 1 and hd + 2 < HPC:
            load_head(hd + 2)
        pk = st - (nst - STG)
        v0 = 0 if pk < 0 else pk * 128
        obank = O_BANKS[gslot]
        sbank_sum = S_BANKS[gslot]
        wsum = [(c_act, M_exp_map.pop(j))]
        if st == 0:
            wsum.append(s_free[gslot])
        ms_ = em.op("pe", lambda t, _st=st, _v0=v0, _n=nst, _j=j,
                    _sb=sbank_sum: t.matmul(
            _sb[0:1, _v0:QG], ones_cb[:, 0:1],
            pT[_j % NPT][:, _v0:QG],
            start=(_st == 0), stop=(_st == _n - 1)),
            waits=wsum, inc=(c_pe if st == nst - 1 else None))
        if st == nst - 1:
            M_sums = ms_
        wpv = [o_free[gslot]] if st == 0 else []
        mpv = em.op("pe", lambda t, _hs=hs, _st=st, _v0=v0, _n=nst, _j=j,
                    _ob=obank: t.matmul(
            _ob[:, _v0:QG], vh[_hs][:, _st, :],
            pT[_j % NPT][:, _v0:QG],
            start=(_st == 0), stop=(_st == _n - 1)),
            waits=wpv, inc=c_pe)
        pT_free[j % NPT] = (c_pe, mpv)
        if st == nst - 1:
            M_pv = mpv
            mrec = em.op("dve", lambda v, _sb=sbank_sum, _g=gslot:
                         v.reciprocal(
                rrow2[_g][0:1, 0:QG], _sb[0:1, 0:QG]),
                waits=[(c_pe, M_sums), rrow_free[gslot],
                       (c_dve, c_dve.n)], inc=c_dve)
            s_free[gslot] = (c_dve, mrec)

            def make_ep(_hd=hd, _qg=qg, _g=gslot, _ob=obank, _mrec=mrec,
                        _M_pv=M_pv, _sb=sbank_sum):
                def ep():
                    mbc = em.op("pe", lambda t: t.matmul(
                        _sb[:, 0:QG],
                        ones_rf[0:1, :], rrow2[_g][0:1, 0:QG],
                        start=True, stop=True),
                        waits=[(c_dve, _mrec), (c_pool, M_consts)], inc=c_pe)
                    rrow_free[_g] = (c_pe, mbc)
                    mbcc = em.op("dve", lambda v: v.tensor_copy(
                        brd2[_g][:, 0:QG], _sb[:, 0:QG]),
                        waits=[(c_pe, mbc), brd_free[_g]], inc=c_dve)
                    s_free[_g] = (c_dve, mbcc)
                    asl = as_i[0] % 4
                    as_i[0] += 1
                    mnrm = em.op("dve", lambda v, _a=asl: v.tensor_tensor(
                        aost[_a][:, :], _ob[:, 0:QG], brd2[_g][:, 0:QG],
                        ALU.mult),
                        waits=[(c_dve, mbcc), (c_pe, _M_pv), aost_free[asl]],
                        inc=c_dve)
                    o_free[_g] = (c_dve, mnrm)
                    brd_free[_g] = (c_dve, mnrm)

                    def do_store(_a=asl, _h2=_hd, _q2=_qg, _mn=mnrm):
                        mast = em.op("act", lambda s: s.dma_start(
                            out=a_in[_h2 * 128:(_h2 + 1) * 128,
                                     _q2 * QG:(_q2 + 1) * QG],
                            in_=aost[_a][:, :]),
                            waits=[(c_dve, _mn)], inc=d_as[_a], inc_k=16)
                        aost_free[_a] = (d_as[_a], mast)
                        if (_h2 == HPC // 2 - 1 and _q2 == NQG - 1
                                and HPC >= 2):
                            M_ast_half[0] = [(d_as[k], d_as[k].n)
                                             for k in range(4)]
                    pending_st.append(do_store)
                return ep
            pending_ep.append(make_ep())
            if qg == NQG - 1:
                head_slot_free[hs] = (c_pe, mpv)
                if hd == 0 and hd + 2 < HPC:
                    load_head(hd + 2)
    flush_ep()
    flush_ep()
    pE_free = o_free[0]

    M_ast = [(d_as[k], d_as[k].n) for k in range(4)]
    groups = [[0, 1, 2, 3], [4, 5, 6, 7]]
    if HPC >= 2:
        em.op("pool", lambda g: g.collective_compute(
            "AllGather", ALU.bypass, replica_groups=groups,
            ins=[a_in[0:Dc2, :].opt()], outs=[ag_outs[0].ap().opt()]),
            waits=M_ast_half[0], inc=c_cc)
        emit_prestage(fw, fwT_d)
        em.op("pool", lambda g: g.collective_compute(
            "AllGather", ALU.bypass, replica_groups=groups,
            ins=[a_in[Dc2:, :].opt()], outs=[ag_outs[1].ap().opt()]),
            waits=M_ast, inc=c_cc)
    else:
        emit_prestage(fw, fwT_d)
        em.op("pool", lambda g: g.collective_compute(
            "AllGather", ALU.bypass, replica_groups=groups,
            ins=[a_in.ap().opt()], outs=[ag_outs[0].ap().opt()]),
            waits=M_ast, inc=c_cc)

    em.barrier(exclude=(c_cc,))
    free_phase()

    aT = sbf("aT", [128, CC, TH], BF16)
    x1 = sbf("x1", [128, TTH, C], F32)
    DTL = DB3 // 128
    w3T = [sbf("w3T0", [128, DTL, CC, 128], BF16),
           sbf("w3T1", [128, DTL, CC, 128], BF16)]
    x2T = aT
    x2bf = [sbf("x2bfa", [128, C], BF16), sbf("x2bfb", [128, C], BF16)]
    bias_b = sbf("bias_b", [128, C], F32)
    stt_t = sbf("stt", [128, 8, 8], F32)
    sfb = sbf("sfb", [1, C], F16)
    em.op("pool", lambda g: g.dma_start(out=sfb[:, :], in_=fb[:, :]),
          inc=d_fb, inc_k=16)

    pE_free3 = pE_free
    last_bb = None
    WW = min(512, C)
    for j in range(C // WW):
        mb = em.op("pe", lambda t, _j=j: t.matmul(
            pE[:, 0:WW], ones_rh[0:1, :], sfb[0:1, _j * WW:(_j + 1) * WW],
            start=True, stop=True),
            waits=[(d_fb, 16), pE_free3, (c_pool, M_consts)], inc=c_pe)
        last_bb = em.op("dve", lambda v, _j=j: v.tensor_copy(
            bias_b[:, _j * WW:(_j + 1) * WW], pE[:, 0:WW]),
            waits=[(c_pe, mb)], inc=c_dve)
        pE_free3 = (c_dve, last_bb)
    M_bias = (c_dve, last_bb)

    BANKS3 = [pA, pB, pC, pD, pE, pF, pG, pH]
    w3T_free = [(c_pe, 0), (c_pe, 0)]
    aT_free = (c_cc, 0)
    x1_free = (d_out, 0)
    x2bf_free = [(d_x2t, 0), (c_act, 0)]
    reg_free = {(tt, par): (c_dve, last_bb)
                for tt in range(TTH) for par in range(2)}
    state = {"M_xl": 0, "x2T_ready": {}}

    def sweep(hf, src_w, srcT, add_bias, first_sweep, at_marks=None):
        if first_sweep and NCCL == 2:
            cc_order = ([cc for cc in range(CC) if (cc % (2 * CCL2)) < CCL2]
                        + [cc for cc in range(CC) if (cc % (2 * CCL2)) >= CCL2])
        else:
            cc_order = list(range(CC))
        marks = {}
        for db in range(NDB):
            ws = db % 2
            wl = [w3T_free[ws]] + [
                pre_marks[(id(src_w), dt)]
                for dt in range(db * (DB3 // 128), (db + 1) * (DB3 // 128))]
            mt = em.op("sp", lambda e, _db=db, _ws=ws, _w=src_w: e.dma_start(
                out=w3T[_ws][:, :, :, :],
                in_=_w[_db * DTL:(_db + 1) * DTL].rearrange(
                    "dt p cc dsub -> p dt cc dsub")),
                waits=wl, inc=d_w3t[ws], inc_k=16)
            M_w = [(d_w3t[ws], mt)]
            par = db % 2
            mm_tt = {}
            for tt in range(TTH):
                for ci, cc in enumerate(cc_order):
                    bank = BANKS3[tt * 2 + par]
                    w = []
                    if ci == 0:
                        w = M_w + [reg_free[(tt, par)]]
                        if first_sweep:
                            w.append(at_marks[0])
                        else:
                            w.append(state["x2T_ready"][tt])
                    if first_sweep and NCCL == 2 and ci == CC // 2:
                        w = w + [at_marks[1]]
                    mm = em.op("pe", lambda t, _cc=cc, _tt=tt, _ws=ws,
                               _ci=ci, _bank=bank, _sT=srcT: t.matmul(
                        _bank[:, 0:DB3],
                        _sT[:, _cc, _tt * 128:(_tt + 1) * 128],
                        w3T[_ws][:, :, _cc, :],
                        start=(_ci == 0), stop=(_ci == CC - 1)),
                        waits=w, inc=(c_pe if ci == CC - 1 else None))
                    if ci == CC - 1:
                        mm_tt[tt] = mm
            w3T_free[ws] = (c_pe, mm_tt[TTH - 1])
            for tt in range(TTH):
                bank = BANKS3[tt * 2 + par]
                me = em.op("dve", lambda v, _tt=tt, _db=db, _bank=bank,
                           _par=par: v.tensor_tensor(
                    x1[:, _tt, _db * DB3:(_db + 1) * DB3],
                    _bank[:, 0:DB3],
                    x1[:, _tt, _db * DB3:(_db + 1) * DB3], ALU.add),
                    waits=[(c_pe, mm_tt[tt]), (d_xl, state["M_xl"]),
                           (c_dve, c_dve.n)], inc=c_dve)
                if add_bias:
                    me = em.op("dve", lambda v, _tt=tt, _db=db: v.tensor_tensor(
                        x1[:, _tt, _db * DB3:(_db + 1) * DB3],
                        x1[:, _tt, _db * DB3:(_db + 1) * DB3],
                        bias_b[:, _db * DB3:(_db + 1) * DB3], ALU.add),
                        waits=[(c_dve, me), M_bias], inc=c_dve)
                reg_free[(tt, par)] = (c_dve, me)
                marks[tt] = me
        return marks

    def layernorm(tt, evac_mark, li):
        stt = stt_t[:, (li % 2) * 4 + tt, :]
        xs = x2bf[tt % 2]
        mss = em.op("dve", lambda v, _tt=tt, _s=stt: v.reduce_sum(
            _s[:, 0:1], x1[:, _tt, :], axis=AX.X),
            waits=[(c_dve, evac_mark)], inc=c_dve)
        msq = em.op("act", lambda s, _tt=tt, _s=stt, _xs=xs: s.activation(
            _xs[:, :], x1[:, _tt, :], AF.Square, accum_out=_s[:, 2:3]),
            waits=[(c_dve, evac_mark), x2bf_free[tt % 2]], inc=c_act)
        mm_ = em.op("dve", lambda v, _s=stt: v.tensor_scalar_mul(
            _s[:, 1:2], _s[:, 0:1], kconst[:, 2:3]), waits=[(c_dve, mss)],
            inc=c_dve)
        me2 = em.op("dve", lambda v, _s=stt: v.tensor_scalar_mul(
            _s[:, 3:4], _s[:, 2:3], kconst[:, 2:3]), waits=[(c_act, msq)],
            inc=c_dve)
        mq2 = em.op("dve", lambda v, _s=stt: v.tensor_tensor(
            _s[:, 4:5], _s[:, 1:2], _s[:, 1:2], ALU.mult),
            waits=[(c_dve, mm_)], inc=c_dve)
        mvar = em.op("dve", lambda v, _s=stt: v.tensor_tensor(
            _s[:, 5:6], _s[:, 3:4], _s[:, 4:5], ALU.subtract),
            waits=[(c_dve, mq2), (c_dve, me2)], inc=c_dve)
        msd = em.op("act", lambda s, _s=stt: s.activation(
            _s[:, 6:7], _s[:, 5:6], AF.Sqrt, bias=kconst[:, 1:2]),
            waits=[(c_dve, mvar)], inc=c_act)
        mrs = em.op("dve", lambda v, _s=stt: v.reciprocal(
            _s[:, 7:8], _s[:, 6:7]), waits=[(c_act, msd)], inc=c_dve)
        mapp = em.op("dve", lambda v, _tt=tt, _s=stt: v.tensor_scalar(
            x1[:, _tt, :], x1[:, _tt, :], _s[:, 1:2], _s[:, 7:8],
            ALU.subtract, ALU.mult), waits=[(c_dve, mrs)], inc=c_dve)
        return mapp

    NCCL = 2 if HPC >= 2 else 1
    CCL2 = Dc2 // 128
    for hf in range(NHF):
        toff = hf * TH
        at_marks = []
        for part in range(NCCL):
            em.wait("act", [(c_cc, part + 1)] + ([aT_free] if part == 0
                                                 else []))
            nldm = 1 if NCCL == 1 else CCL2
            d_at[part].n += 16 * nldm
            at_marks.append((d_at[part], d_at[part].n))

            def _run_at(s, _hf=hf, _sem=d_at[part], _part=part, _n=nldm):
                for arm in nc.switch_core_id(engines=[s], n=n_arms):
                    g = (arm.logical % N_CORES) % CPB
                    off = g * TPC + _hf * TH
                    if NCCL == 1:
                        s.dma_start(
                            out=aT[:, :, :],
                            in_=ag_outs[0].ap().rearrange(
                                "(cc p) t -> p cc t", p=128)
                            [:, :, off:off + TH]).then_inc(_sem.h, 16)
                    else:
                        for cq in range(_n):
                            s.dma_start(
                                out=aT.ap().rearrange(
                                    "p (r ccl) t -> p r ccl t", ccl=2 * CCL2)
                                [:, :, _part * CCL2 + cq, :],
                                in_=ag_outs[_part].ap().rearrange(
                                    "(rr cq p) t -> p rr cq t",
                                    p=128, cq=CCL2)
                                [:, :, cq, off:off + TH]).then_inc(_sem.h, 16)
            em.raw("act", _run_at)

        M_xl = None
        for tt in range(TTH):
            M_xl = em.op("act", lambda s, _tt=tt, _to=toff: s.dma_start(
                out=x1[:, _tt, :],
                in_=x_slice[_to + _tt * 128:_to + (_tt + 1) * 128, :]),
                waits=([x1_free] if tt == 0 else []), inc=d_xl, inc_k=16)
        state["M_xl"] = M_xl

        ev1 = sweep(hf, woT_d, aT, False, True, at_marks)
        sweep1_done = (c_pe, c_pe.n)
        state["x2T_ready"] = {}
        for tt in range(TTH):
            mapp = layernorm(tt, ev1[tt], hf * 2)
            mcast = em.op("act", lambda s, _tt=tt, _xs=x2bf[tt % 2]: s.copy(
                _xs[:, :], x1[:, _tt, :]),
                waits=[(c_dve, mapp), x2bf_free[tt % 2]], inc=c_act)
            mx2t = em.op("sp", lambda e, _tt=tt, _xs=x2bf[tt % 2]: e.dma_start(
                out=x2T[:, :, _tt * 128:(_tt + 1) * 128], in_=_xs[:, :],
                transpose=True),
                waits=[(c_act, mcast), sweep1_done],
                inc=d_x2t, inc_k=16)
            x2bf_free[tt % 2] = (d_x2t, mx2t)
            state["x2T_ready"][tt] = (d_x2t, mx2t)
        ev2 = sweep(hf, fwT_d, x2T, True, False)
        aT_free = (c_pe, c_pe.n)
        last_out = None
        for tt in range(TTH):
            mapp2 = layernorm(tt, ev2[tt], hf * 2 + 1)
            last_out = em.op("act", lambda s, _tt=tt, _to=toff: s.dma_start(
                out=out[_to + _tt * 128:_to + (_tt + 1) * 128, :],
                in_=x1[:, _tt, :]),
                waits=[(c_dve, mapp2)], inc=d_out, inc_k=16)
        x1_free = (d_out, last_out)

    if debug_taps:
        dbg = {}
        for nm, src_t, shp in (("dbg_q", q_scr, [HPC * 128, T]),
                               ("dbg_k", k_scr, [HPC * 128, T]),
                               ("dbg_v", v_scr, [HPC * T, 128]),
                               ("dbg_a", a_in, [Dc, T]),
                               ("dbg_ag", ag_out, [C, T])):
            o_t = nc.dram_tensor(nm, shp, BF16, kind="ExternalOutput")
            em.op("pool", lambda g, _o=o_t, _s=src_t, _shp=shp: g.dma_start(
                out=_o.ap(), in_=_s.ap().rearrange(
                    "a b c -> (a b) c") if len(_s.shape) == 3 else _s.ap()),
                waits=[(d_out, d_out.n)], inc=d_fb, inc_k=16)
        em.wait("pool", [(d_fb, d_fb.n)])
    em.wait("act", [(d_out, d_out.n)])
    em.wait("pool", [(d_out, d_out.n)])

    for c in em.ctrs.values():
        c.h = alloc(nc.semaphore(c.name), persist_ctx)
    with nc.Block() as block:
        @block.tensor
        def _(t):
            for fn in em.q["pe"]:
                fn(t)

        @block.vector
        def _(v):
            for fn in em.q["dve"]:
                fn(v)

        @block.scalar
        def _(s):
            for fn in em.q["act"]:
                fn(s)

        @block.gpsimd
        def _(g):
            for fn in em.q["pool"]:
                fn(g)

        @block.sync
        def _(e):
            for fn in em.q["sp"]:
                fn(e)

    free_phase()
    while persist_ctx:
        persist_ctx.pop().__exit__(None, None, None)
    return nc


def _rope_tables(T):
    h = HDIM
    pos = np.arange(T, dtype=np.float32)
    thetas = (1.0 / (np.float32(10000.0) **
                     (np.arange(0, h, 2, dtype=np.float32) / np.float32(h))))
    pt = np.outer(pos, thetas.astype(np.float32)).astype(np.float32)
    pt = np.concatenate([pt, pt], axis=-1)
    sin_t = np.sin(pt).astype(np.float32).T.copy()
    cos_t = np.cos(pt).astype(np.float32).T.copy()
    sin_t[0:h // 2] = -sin_t[0:h // 2]
    return np.ascontiguousarray(sin_t), np.ascontiguousarray(cos_t)


def kernel(L0_attn_x, L0_attn_wq, L0_attn_wk, L0_attn_wv, L0_attn_wo,
           L0_ffn_weight, L0_ffn_bias):
    x = np.ascontiguousarray(np.asarray(L0_attn_x, dtype=np.float32))
    wq = np.ascontiguousarray(np.asarray(L0_attn_wq, dtype=np.float32))
    wk = np.ascontiguousarray(np.asarray(L0_attn_wk, dtype=np.float32))
    wv = np.ascontiguousarray(np.asarray(L0_attn_wv, dtype=np.float32))
    wo = np.ascontiguousarray(np.asarray(L0_attn_wo, dtype=np.float32))
    fw = np.ascontiguousarray(np.asarray(L0_ffn_weight, dtype=np.float16))
    fb = np.ascontiguousarray(
        np.asarray(L0_ffn_bias, dtype=np.float16).reshape(1, -1))
    Bsz, T, C = x.shape
    assert Bsz == B
    TPC = T // CPB
    Dc = C // CPB
    rsin, rcos = _rope_tables(T)

    nc = build_program(T, C)

    in_maps = []
    for c in range(N_CORES):
        b, g = c // CPB, c % CPB
        in_maps.append({
            "x_full": x[b],
            "x_slice": np.ascontiguousarray(x[b, g * TPC:(g + 1) * TPC]),
            "wq": np.ascontiguousarray(wq[g * Dc:(g + 1) * Dc]),
            "wk": np.ascontiguousarray(wk[g * Dc:(g + 1) * Dc]),
            "wv": np.ascontiguousarray(wv[g * Dc:(g + 1) * Dc]),
            "wo": wo,
            "fw": fw,
            "fb": fb,
            "rsin": rsin,
            "rcos": rcos,
        })
    res = run_bass_kernel_spmd(nc, in_maps, core_ids=list(range(N_CORES)))
    outp = np.empty((B, T, C), dtype=np.float32)
    for c in range(N_CORES):
        b, g = c // CPB, c % CPB
        outp[b, g * TPC:(g + 1) * TPC] = res.results[c]["out"]
    return outp
